# version 10
# baseline (speedup 1.0000x reference)
import numpy as np
import ml_dtypes
import concourse.bass as bass
import concourse.mybir as mybir
from concourse.bass_utils import run_bass_kernel_spmd
from contextlib import ExitStack

F32 = mybir.dt.float32
BF16 = mybir.dt.bfloat16
I32 = mybir.dt.int32
AF = mybir.ActivationFunctionType
ALU = mybir.AluOpType
AX = mybir.AxisListType

D = 1024
DH = 4096
DA = 2048

PI_F = float(np.float32(np.pi))
TWO_PI_F = 2.0 * PI_F
INV2PI = float(np.float32(1.0 / (2.0 * np.pi)))
_c1 = np.float32(6.28125)
_c2 = np.float32(2.0 * np.pi - float(_c1))
_c3 = np.float32(2.0 * np.pi - float(_c1) - float(_c2))
CW1 = float(_c1)
CW2 = float(np.float32(float(_c2) + float(_c3)))
SQRT5 = float(np.float32(np.sqrt(5.0)))
EPS_ACC = float(np.float32(1e-8) / np.float32(5.0))


class CFG:
    def __init__(self, T=2048, M=1024, XC=256, MIDC=512, NMOV=512):
        self.T, self.M, self.XC, self.MIDC, self.NMOV = T, M, XC, MIDC, NMOV
        assert T == 2 * M and M % XC == 0 and M % MIDC == 0 and M % NMOV == 0
        assert 4 * XC == M, "xstage/ostage region formulas assume XC == M/4"
        self.NP = T // M
        self.NCH = M // NMOV
        assert self.NCH == 2, "psum bank plan assumes 2 moving chunks per pass"
        assert 24 * MIDC + 8 * XC + max(8 * XC, NMOV) <= 16 * M


class Buf:
    __slots__ = ("w", "r", "name")

    def __init__(self, name=""):
        self.w = {}
        self.r = {}
        self.name = name


def _acc(d, tok):
    s, v = tok
    if d.get(s, 0) < v:
        d[s] = v


def build_program(cfg: CFG):
    T, M, XC, MIDC, NMOV = cfg.T, cfg.M, cfg.XC, cfg.MIDC, cfg.NMOV
    nc = bass.Bass("TRN2", target_bir_lowering=False)

    dr = {}
    dr["x_fm"] = nc.dram_tensor("x_fm", [T // XC, 128 * 8 * XC], F32,
                                kind="ExternalInput")
    dr["xp_fm"] = nc.dram_tensor("xp_fm", [T // XC, 128 * 8 * XC], F32,
                                 kind="ExternalInput")
    dr["pos_o"] = nc.dram_tensor("pos_o", [128, T], F32, kind="ExternalInput")
    dr["pos_p"] = nc.dram_tensor("pos_p", [128, T], F32, kind="ExternalInput")
    dr["mask"] = nc.dram_tensor("mask", [128, 1], F32, kind="ExternalInput")
    for nm, shp in [("wmag_t", [D, D]), ("wq_t", [D, D]), ("w1_t", [DA, DH]),
                    ("w2_t", [D, DA]), ("fw1_t", [DH, D]), ("fw2_t", [DH, D]),
                    ("h1_t", [DH, DH]), ("h2_t", [DH, DH]), ("w3_t", [D, DH])]:
        dr[nm] = nc.dram_tensor(nm, shp, F32, kind="ExternalInput")
    VEC_SHAPES = [("v_bmag", 8), ("v_bq", 8), ("v_b2", 8), ("v_rp", 8),
                  ("v_s3", 8), ("v_b1", 16), ("v_lng", 32), ("v_lnb", 32),
                  ("v_s1", 32), ("v_s2", 32), ("v_h1s", 32), ("v_h2s", 32)]
    for nm, w in VEC_SHAPES:
        dr[nm] = nc.dram_tensor(nm, [128, w], F32, kind="ExternalInput")
    dr["ones2"] = nc.dram_tensor("ones2", [33, 128], F32, kind="ExternalInput")
    dr["sel_a"] = nc.dram_tensor("sel_a", [128, 33], BF16, kind="ExternalInput")
    dr["sel_b"] = nc.dram_tensor("sel_b", [128, 33], BF16, kind="ExternalInput")
    dr["y_fm"] = nc.dram_tensor("y_fm", [T // XC, 128 * 8 * XC], F32,
                                kind="ExternalOutput")

    es = ExitStack()
    sbt = lambda name, shp, dt: es.enter_context(nc.sbuf_tensor(name, shp, dt))

    big1 = sbt("big1", [128, 32 * M], BF16)
    big2 = sbt("big2", [128, 32 * M], BF16)
    obuf = sbt("obuf", [128, 8 * M], F32)
    xn_hi = sbt("xn_hi", [128, 8 * M], BF16)
    wstage = sbt("wstage", [128, 2 * 4096], BF16)
    lnbmu = sbt("lnbmu", [128, T], BF16)
    lnbr = sbt("lnbr", [128, T], F32)

    carries = sbt("carries", [128, 24], F32)
    epsc = sbt("epsc", [128, 2], F32)
    vec = {nm: sbt("s" + nm, [128, w], F32) for nm, w in VEC_SHAPES}
    mask_sb = sbt("mask_sb", [128, 1], F32)
    ones2 = sbt("ones2_sb", [33, 128], F32)
    sel_a = sbt("sel_a_sb", [128, 33], BF16)
    sel_b = sbt("sel_b_sb", [128, 33], BF16)

    big1f = big1.bitcast(F32)
    big2f = big2.bitcast(F32)
    big2i = big2.bitcast(I32)
    obuf_bf = obuf.bitcast(BF16)

    psum = [es.enter_context(nc.psum_tensor(f"ps{i}", [128, 512], F32))
            for i in range(8)]

    sems = {nm: es.enter_context(nc.semaphore(nm))
            for nm in ["LDV", "LDS0", "LDS1", "LDP", "LDW0", "LDW1", "LDW2", "LDW3",
                       "STS0", "STS1", "VS", "SS", "TS"]}

    XSB = 32 * XC
    B1REG = [Buf(f"b1_{i}") for i in range(32)]
    big1_regs = [(i * 2 * M, (i + 1) * 2 * M, B1REG[i]) for i in range(32)]
    B_XS = [B1REG[0:4], B1REG[4:8]]
    B_OS = [B1REG[8:12], B1REG[12:16]]

    B2REG = [Buf(f"b2_{i}") for i in range(16)]
    B_U = [Buf(f"u{i}") for i in range(12)]
    B_SQ, B_XB = Buf("sq"), Buf("xb")
    MIDF0 = 8 * M
    SQ_BF = 32 * M - 8 * XC - max(8 * XC, NMOV)
    XBF_BF = 32 * M - 8 * XC
    big2_regs = [(i * 2 * M, (i + 1) * 2 * M, B2REG[i]) for i in range(16)]
    for i in range(12):
        lo = (MIDF0 + i * MIDC) * 4
        big2_regs.append((lo, lo + MIDC * 4, B_U[i]))
    big2_regs.append((SQ_BF * 2, XBF_BF * 2, B_SQ))
    big2_regs.append((XBF_BF * 2, 64 * M, B_XB))

    B_OB0, B_OBI, B_OBR = Buf("ob0"), Buf("obi"), Buf("obr")
    obuf_regs = [(0, 4 * M, B_OB0), (4 * M, 20 * M, B_OBI), (20 * M, 32 * M, B_OBR)]

    def regs_in(regs, lo, hi):
        return [b for (a, z, b) in regs if a < hi and lo < z]

    def b1r(lo_bf, hi_bf):
        return regs_in(big1_regs, lo_bf * 2, hi_bf * 2)

    def b2r(lo_bf, hi_bf):
        return regs_in(big2_regs, lo_bf * 2, hi_bf * 2)

    def obr(lo_f32, hi_f32):
        return regs_in(obuf_regs, lo_f32 * 4, hi_f32 * 4)

    B_XN = Buf("xn")
    B_WS4 = [Buf(f"ws4_{i}") for i in range(4)]
    B_WS = [B_WS4[0:2], B_WS4[2:4]]
    B_LNM, B_LNR = Buf("lnbmu"), Buf("lnbr")
    B_SROW = [B_U[8], B_U[9]]
    B_CR, B_CI, B_CM = Buf("cr"), Buf("ci"), Buf("cm")
    B_PS = [Buf(f"ps{i}") for i in range(8)]
    B_VEC = {nm: Buf(nm) for nm, _ in VEC_SHAPES}
    B_MASK, B_EPS = Buf("mask"), Buf("eps")
    B_O2, B_SELA, B_SELB = Buf("o2"), Buf("sela"), Buf("selb")

    ENG = ("sync", "gpsimd", "vector", "scalar", "tensor")
    q = {e: [] for e in ENG}
    cnt = {"VS": 0, "SS": 0, "TS": 0, "LDV": 0, "LDS0": 0, "LDS1": 0,
           "LDP": 0, "LDW0": 0, "LDW1": 0, "LDW2": 0, "LDW3": 0,
           "STS0": 0, "STS1": 0}
    hw = {e: {} for e in ENG}
    CHAIN = {"vector": "VS", "scalar": "SS", "tensor": "TS"}

    def _mkwaits(eng, deps):
        out = []
        for s, v in deps:
            if hw[eng].get(s, 0) >= v:
                continue
            hw[eng][s] = v
            out.append((s, v))
        return out

    def _deps(reads, writes, extra):
        deps = list(extra)
        for b in reads:
            deps += b.w.items()
        for b in writes:
            deps += b.w.items()
            deps += b.r.items()
        return deps

    def op(eng, fn, reads=(), writes=(), extra=()):
        sem = CHAIN[eng]
        waits = _mkwaits(eng, _deps(reads, writes, extra))
        cnt[sem] += 1
        tok = (sem, cnt[sem])

        def cl(e, fn=fn, waits=waits, sem=sem):
            for s, v in waits:
                e.wait_ge(sems[s], v)
            fn(e).then_inc(sems[sem], 1)
        q[eng].append(cl)
        for b in reads:
            _acc(b.r, tok)
        for b in writes:
            _acc(b.w, tok)
        return tok

    def dop(queue, sem, out_ap, in_ap, reads=(), writes=(), extra=()):
        waits = _mkwaits(queue, _deps(reads, writes, extra))
        cnt[sem] += 16
        tok = (sem, cnt[sem])

        def cl(e, out_ap=out_ap, in_ap=in_ap, waits=waits, sem=sem):
            for s, v in waits:
                e.wait_ge(sems[s], v)
            e.dma_start(out=out_ap, in_=in_ap).then_inc(sems[sem], 16)
        q[queue].append(cl)
        for b in reads:
            _acc(b.r, tok)
        for b in writes:
            _acc(b.w, tok)
        return tok

    V = lambda fn, reads=(), writes=(), extra=(): op("vector", fn, reads, writes, extra)
    S = lambda fn, reads=(), writes=(), extra=(): op("scalar", fn, reads, writes, extra)
    TE = lambda fn, reads=(), writes=(), extra=(): op("tensor", fn, reads, writes, extra)

    def ap3(tensor, off, dims):
        return bass.AP(tensor, off, [list(d) for d in dims])

    V(lambda e: e.memset(carries[:], 0.0), writes=[B_CR, B_CI, B_CM])
    V(lambda e: e.memset(epsc[:, 0:1], 1e-5), writes=[B_EPS])
    V(lambda e: e.memset(epsc[:, 1:2], PI_F / 2), writes=[B_EPS])

    _init_bufs = [B_MASK, B_O2, B_SELA, B_SELB] + list(B_VEC.values())
    for nm, _w in VEC_SHAPES:
        dop("sync", "LDV", vec[nm][:], dr[nm].ap(), writes=[B_VEC[nm]])
    dop("sync", "LDV", mask_sb[:], dr["mask"].ap(), writes=[B_MASK])
    dop("sync", "LDV", ones2[:], dr["ones2"].ap(), writes=[B_O2])
    dop("sync", "LDV", sel_a[:], dr["sel_a"].ap(), writes=[B_SELA])
    dop("sync", "LDV", sel_b[:], dr["sel_b"].ap(), writes=[B_SELB])
    for _b in _init_bufs:
        _b.w = {"LDV": cnt["LDV"]}

    def midu(u, n=None):
        n = MIDC if n is None else n
        return big2f[:, MIDF0 + u * MIDC: MIDF0 + u * MIDC + n]

    def midu_i32(n=None):
        n = MIDC if n is None else n
        return big2i[:, MIDF0 + 11 * MIDC: MIDF0 + 11 * MIDC + n]

    def midu_bf(u, n=None):
        n = MIDC if n is None else n
        return big2[:, 2 * (MIDF0 + u * MIDC): 2 * (MIDF0 + u * MIDC) + n]

    def xst_ap(buf, n_tok):
        return ap3(big1f, buf * (8 * XC), [[16 * M, 128], [n_tok, 8], [1, n_tok]])

    def xst_blk(buf, b, n_tok):
        return ap3(big1f, buf * (8 * XC) + b * n_tok, [[16 * M, 128], [1, n_tok]])

    def xdram_ap(xdram, off, n_tok):
        assert n_tok == XC and off % XC == 0
        c = off // XC
        return ap3(xdram, c * (128 * 8 * XC), [[8 * XC, 128], [1, 8 * XC]])

    def wslice_ap(wdram, m, IN, OUT):
        return ap3(wdram, m * 128 * IN, [[IN, 128], [1, IN]])

    def wstage_ap(buf, K):
        return ap3(wstage, buf * 4096, [[2 * 4096, 128], [128, K], [1, 128]])

    def stat_pipeline(csize, dim, sum_reads, mk_sums,
                      mu_out=None, r_out=None, v_out=None, sums_bank=4):
        par = (sums_bank % 2) if csize <= 256 else 0
        sbase = MIDF0 + (8 + par) * MIDC
        bcb = (6, 7) if par == 0 else (0, 1)
        if mk_sums is not None:
            TE(mk_sums, reads=list(sum_reads) + [B_SELA, B_SELB],
               writes=[B_PS[sums_bank]])
        V(lambda e, sums_bank=sums_bank, sbase=sbase: e.tensor_copy(
            out=big2f[0:33, sbase:sbase + csize],
            in_=psum[sums_bank][0:33, 0:csize]),
          reads=[B_PS[sums_bank]], writes=[B_SROW[par]])

        def bc(e, sbase=sbase, bcb=bcb):
            e.matmul(psum[bcb[0]][:, 0:csize], ones2[0:1, :],
                     big2f[0:1, sbase:sbase + csize], start=True, stop=True)
            return e.matmul(psum[bcb[1]][:, 0:csize], ones2[32:33, :],
                            big2f[32:33, sbase:sbase + csize],
                            start=True, stop=True)
        TE(bc, reads=[B_SROW[par], B_O2], writes=[B_PS[bcb[0]], B_PS[bcb[1]]])
        V(lambda e, bcb=bcb: e.tensor_scalar(
            out=midu(0, csize), in0=psum[bcb[0]][:, 0:csize],
            scalar1=1.0 / dim, scalar2=None, op0=ALU.mult),
          reads=[B_PS[bcb[0]]], writes=[B_U[0]])
        V(lambda e: e.tensor_tensor(out=midu(1, csize), in0=midu(0, csize),
                                    in1=midu(0, csize), op=ALU.mult),
          reads=[B_U[0]], writes=[B_U[1]])
        V(lambda e, bcb=bcb: e.tensor_scalar(
            out=midu(2, csize), in0=psum[bcb[1]][:, 0:csize],
            scalar1=1.0 / dim, scalar2=None, op0=ALU.mult),
          reads=[B_PS[bcb[1]]], writes=[B_U[2]])
        V(lambda e: e.tensor_tensor(out=midu(1, csize), in0=midu(2, csize),
                                    in1=midu(1, csize), op=ALU.subtract),
          reads=[B_U[2], B_U[1]], writes=[B_U[1]])
        S(lambda e: e.activation(out=midu(2, csize), in_=midu(1, csize),
                                 func=AF.Ln, bias=epsc[:, 0:1], scale=1.0),
          reads=[B_U[1], B_EPS], writes=[B_U[2]])
        S(lambda e: e.activation(out=midu(3, csize), in_=midu(2, csize),
                                 func=AF.Exp, scale=-0.5),
          reads=[B_U[2]], writes=[B_U[3]])
        if r_out is not None:
            ap_, bufs_ = r_out
            V(lambda e, ap_=ap_: e.tensor_copy(out=ap_, in_=midu(3, csize)),
              reads=[B_U[3]], writes=bufs_)
        if mu_out is not None:
            ap_, bufs_ = mu_out
            V(lambda e, ap_=ap_: e.tensor_copy(out=ap_, in_=midu(0, csize)),
              reads=[B_U[0]], writes=bufs_)
        if v_out is not None:
            ap_, bufs_ = v_out
            V(lambda e, ap_=ap_: e.tensor_tensor(out=ap_, in0=midu(0, csize),
                                                 in1=midu(3, csize), op=ALU.mult),
              reads=[B_U[0], B_U[3]], writes=bufs_)

    def stats_chunk(xdram, off, csize, buf, mu_out, r_out):
        ldx = dop("sync", f"LDS{buf}", xst_ap(buf, csize),
                  xdram_ap(xdram, off, csize), writes=B_XS[buf])
        if buf == 0:
            xbf0, sq0 = XBF_BF, SQ_BF
            xb_bufs, sq_bufs = [B_XB], [B_SQ]
        else:
            xbf0 = 2 * (MIDF0 + 4 * MIDC)
            sq0 = 2 * (MIDF0 + 6 * MIDC)
            xb_bufs, sq_bufs = [B_U[4], B_U[5]], [B_U[6], B_U[7]]
        bank = 4 + buf
        V(lambda e, buf=buf, xbf0=xbf0: e.tensor_copy(
            out=big2[:, xbf0:xbf0 + 8 * csize],
            in_=xst_ap(buf, csize)),
          reads=B_XS[buf], writes=xb_bufs, extra=[ldx])
        S(lambda e, buf=buf, sq0=sq0: e.activation(
            out=big2[:, sq0:sq0 + 8 * csize],
            in_=xst_ap(buf, csize), func=AF.Square),
          reads=B_XS[buf], writes=sq_bufs, extra=[ldx])

        def mk_sums(e, xbf0=xbf0, sq0=sq0, bank=bank):
            last = None
            for b in range(8):
                last = e.matmul(psum[bank][0:33, 0:csize], sel_a[:],
                                big2[:, xbf0 + b * csize:xbf0 + (b + 1) * csize],
                                start=(b == 0), stop=False)
            for b in range(8):
                last = e.matmul(psum[bank][0:33, 0:csize], sel_b[:],
                                big2[:, sq0 + b * csize:sq0 + (b + 1) * csize],
                                start=False, stop=(b == 7))
            return last
        stat_pipeline(csize, D, xb_bufs + sq_bufs, mk_sums,
                      mu_out=mu_out, r_out=r_out, sums_bank=bank)

    def emit_sincos(u_phi, uA, uB, uC, u_sin, u_cos, phi_reads):
        V(lambda e: e.tensor_scalar(out=midu(uA), in0=midu(u_phi),
                                    scalar1=INV2PI, scalar2=0.5,
                                    op0=ALU.mult, op1=ALU.add),
          reads=[B_U[u_phi]] + phi_reads, writes=[B_U[uA]])
        V(lambda e: e.tensor_copy(out=midu_i32(), in_=midu(uA)),
          reads=[B_U[uA]], writes=[B_U[11]])
        V(lambda e: e.tensor_copy(out=midu(uA), in_=midu_i32()),
          reads=[B_U[11]], writes=[B_U[uA]])
        V(lambda e: e.scalar_tensor_tensor(out=midu(uC), in0=midu(uA),
                                           scalar=-CW1, in1=midu(u_phi),
                                           op0=ALU.mult, op1=ALU.add),
          reads=[B_U[uA], B_U[u_phi]], writes=[B_U[uC]])
        V(lambda e: e.scalar_tensor_tensor(out=midu(uB), in0=midu(uA),
                                           scalar=-CW2, in1=midu(uC),
                                           op0=ALU.mult, op1=ALU.add),
          reads=[B_U[uA], B_U[uC]], writes=[B_U[uB]])
        V(lambda e: e.tensor_scalar(out=midu(uC), in0=midu(uB), scalar1=-PI_F,
                                    scalar2=None, op0=ALU.is_lt),
          reads=[B_U[uB]], writes=[B_U[uC]])
        V(lambda e: e.scalar_tensor_tensor(out=midu(uA), in0=midu(uC),
                                           scalar=TWO_PI_F, in1=midu(uB),
                                           op0=ALU.mult, op1=ALU.add),
          reads=[B_U[uC], B_U[uB]], writes=[B_U[uA]])
        S(lambda e: e.activation(out=midu_bf(u_sin), in_=midu(uA), func=AF.Sin),
          reads=[B_U[uA]], writes=[B_U[u_sin]])
        V(lambda e: e.tensor_scalar(out=midu(uC), in0=midu(uA), scalar1=PI_F / 2,
                                    scalar2=None, op0=ALU.is_gt),
          reads=[B_U[uA]], writes=[B_U[uC]])
        V(lambda e: e.scalar_tensor_tensor(out=midu(uB), in0=midu(uC),
                                           scalar=-TWO_PI_F, in1=midu(uA),
                                           op0=ALU.mult, op1=ALU.add),
          reads=[B_U[uC], B_U[uA]], writes=[B_U[uB]])
        S(lambda e: e.activation(out=midu_bf(u_cos), in_=midu(uB), func=AF.Sin,
                                 bias=epsc[:, 1:2], scale=1.0),
          reads=[B_U[uB], B_EPS], writes=[B_U[u_cos]])

    def xn_build(xdram, dram_base, mu_ap_fn, r_ap_fn, mu_bufs, r_bufs,
                 dst_fn=None, dst_bufs_fn=None):
        if dst_fn is None:
            dst_fn = lambda b, c: xn_hi[:, b * M + c * XC:b * M + (c + 1) * XC]
            dst_bufs_fn = lambda b, c: [B_XN]
        for c in range(M // XC):
            buf = c % 2
            ldx = dop("sync", f"LDS{buf}", xst_ap(buf, XC),
                      xdram_ap(xdram, dram_base + c * XC, XC), writes=B_XS[buf])
            for b in range(8):
                V(lambda e, b=b, buf=buf, c=c: e.tensor_tensor(
                    out=xst_blk(buf, b, XC), in0=xst_blk(buf, b, XC),
                    in1=mu_ap_fn(c), op=ALU.subtract),
                  reads=B_XS[buf] + mu_bufs, writes=B_XS[buf], extra=[ldx])
                V(lambda e, b=b, c=c, buf=buf: e.tensor_tensor(
                    out=dst_fn(b, c), in0=xst_blk(buf, b, XC),
                    in1=r_ap_fn(c), op=ALU.mult),
                  reads=B_XS[buf] + r_bufs, writes=dst_bufs_fn(b, c))

    def stream_projection(wdram, IN, OUT, mov_fn, mov_bufs_fn, evac_fn, extra=()):
        K = IN // 128
        for m in range(OUT // 128):
            sbuf = m % 2
            dop("gpsimd", f"LDW{2 * sbuf}", wstage_ap(sbuf, K),
                wslice_ap(wdram, m, IN, OUT), writes=B_WS[sbuf])
            p = m % 2
            banks = (2 * p, 2 * p + 1)

            def mm(e, m=m, sbuf=sbuf, K=K, banks=banks):
                last = None
                for k in range(K):
                    wt_ap = wstage[:, sbuf * 4096 + k * 128:sbuf * 4096 + (k + 1) * 128]
                    for n in range(2):
                        last = e.matmul(psum[banks[n]][:, 0:NMOV], wt_ap,
                                        mov_fn(k, n), start=(k == 0), stop=(k == K - 1))
                return last
            mov_bufs = []
            for k in range(K):
                for bb in mov_bufs_fn(k):
                    if bb not in mov_bufs:
                        mov_bufs.append(bb)
            TE(mm, reads=B_WS[sbuf] + mov_bufs,
               writes=[B_PS[banks[0]], B_PS[banks[1]]], extra=extra)
            for n in range(2):
                evac_fn(m, n, banks[n])

    for c in range(T // XC):
        stats_chunk(dr["x_fm"], c * XC, XC, c % 2,
                    (lnbmu[:, c * XC:(c + 1) * XC], [B_LNM]),
                    (lnbr[:, c * XC:(c + 1) * XC], [B_LNR]))

    def mov_xn(k, n):
        return xn_hi[:, k * M + n * NMOV:k * M + (n + 1) * NMOV]

    def mov_xn_bufs(k):
        return [B_XN]

    def emit_A0(pm):
        pbase = pm * M
        xn_build(dr["x_fm"], pbase,
                 lambda c, pbase=pbase: lnbmu[:, pbase + c * XC:pbase + (c + 1) * XC],
                 lambda c, pbase=pbase: lnbr[:, pbase + c * XC:pbase + (c + 1) * XC],
                 [B_LNM], [B_LNR])

    PMU0, PR0 = M, 2 * M

    def mov_xnp_fn(pp):
        def mov(k, n):
            blk = 16 + 8 * pp + k
            return big1[:, blk * M + n * NMOV:blk * M + (n + 1) * NMOV]
        return mov

    def mov_xnp_bufs_fn(pp):
        return lambda k: [B1REG[16 + 8 * pp + k]]

    for pp in range(cfg.NP):
        for c in range(M // XC):
            stats_chunk(dr["xp_fm"], pp * M + c * XC, XC, c % 2,
                        (obuf[:, PMU0 + c * XC:PMU0 + (c + 1) * XC], [B_OBI]),
                        (obuf[:, PR0 + c * XC:PR0 + (c + 1) * XC], [B_OBI]))
        xn_build(dr["xp_fm"], pp * M,
                 lambda c: obuf[:, PMU0 + c * XC:PMU0 + (c + 1) * XC],
                 lambda c: obuf[:, PR0 + c * XC:PR0 + (c + 1) * XC],
                 [B_OBI], [B_OBI],
                 dst_fn=lambda b, c, pp=pp: big1[
                     :, (16 + 8 * pp + b) * M + c * XC:
                     (16 + 8 * pp + b) * M + (c + 1) * XC],
                 dst_bufs_fn=lambda b, c, pp=pp: [B1REG[16 + 8 * pp + b]])

    def emit_wmag_prev(pp):
        def evac_sigp(m, n, bank, pp=pp):
            base = 8 * M * pp
            S(lambda e, m=m, n=n, bank=bank, base=base: e.activation(
                out=big2[:, base + m * M + n * NMOV:base + m * M + n * NMOV + NMOV],
                in_=psum[bank][:, 0:NMOV], func=AF.Sigmoid,
                bias=vec["v_bmag"][:, m:m + 1], scale=1.0),
              reads=[B_PS[bank], B_VEC["v_bmag"]],
              writes=b2r(base + m * M, base + (m + 1) * M))
        stream_projection(dr["wmag_t"], D, D, mov_xnp_fn(pp),
                          mov_xnp_bufs_fn(pp), evac_sigp)

    def emit_lite(pp):
        dop("sync", "LDP", obuf[:, 0:M],
            ap3(dr["pos_p"], pp * M, [[T, 128], [1, M]]), writes=[B_OB0])
        for b in range(8):
            for c in range(M // MIDC):
                sl = slice(c * MIDC, (c + 1) * MIDC)
                gsl = slice(8 * M * pp + b * M + sl.start,
                            8 * M * pp + b * M + sl.stop)
                xnp = big1[:, (16 + 8 * pp + b) * M + sl.start:
                           (16 + 8 * pp + b) * M + sl.stop]
                xnp_buf = [B1REG[16 + 8 * pp + b]]
                V(lambda e, sl=sl, b=b: e.tensor_scalar(
                    out=midu(0), in0=obuf[:, sl], scalar1=vec["v_rp"][:, b:b + 1],
                    scalar2=0.01, op0=ALU.mult, op1=ALU.mult),
                  reads=[B_OB0, B_VEC["v_rp"]], writes=[B_U[0]])
                emit_sincos(0, 1, 2, 3, 4, 6, [])
                V(lambda e, gsl=gsl, xnp=xnp: e.tensor_tensor(
                    out=midu_bf(7), in0=big2[:, gsl], in1=xnp, op=ALU.mult),
                  reads=b2r(gsl.start, gsl.stop) + xnp_buf, writes=[B_U[7]])
                V(lambda e: e.tensor_tensor(
                    out=midu_bf(8), in0=midu_bf(7), in1=midu_bf(6), op=ALU.mult),
                  reads=[B_U[7], B_U[6]], writes=[B_U[8]])
                V(lambda e: e.tensor_scalar(
                    out=midu_bf(2), in0=midu_bf(8), scalar1=1.0, scalar2=None,
                    op0=ALU.mult, op1=ALU.add, accum_out=midu(10, 1)),
                  reads=[B_U[8]], writes=[B_U[2], B_U[10]])
                V(lambda e, b=b: e.tensor_tensor(
                    out=carries[:, b:b + 1], in0=carries[:, b:b + 1],
                    in1=midu(10, 1), op=ALU.add),
                  reads=[B_U[10], B_CR], writes=[B_CR])
                V(lambda e: e.tensor_tensor(
                    out=midu_bf(8), in0=midu_bf(7), in1=midu_bf(4), op=ALU.mult),
                  reads=[B_U[7], B_U[4], B_U[8]], writes=[B_U[8]])
                V(lambda e: e.tensor_scalar(
                    out=midu_bf(2), in0=midu_bf(8), scalar1=1.0, scalar2=None,
                    op0=ALU.mult, op1=ALU.add, accum_out=midu(10, 1)),
                  reads=[B_U[8]], writes=[B_U[2], B_U[10]])
                V(lambda e, b=b: e.tensor_tensor(
                    out=carries[:, 8 + b:9 + b], in0=carries[:, 8 + b:9 + b],
                    in1=midu(10, 1), op=ALU.add),
                  reads=[B_U[10], B_CI], writes=[B_CI])
                V(lambda e, gsl=gsl: e.tensor_scalar(
                    out=midu_bf(2), in0=big2[:, gsl], scalar1=1.0, scalar2=None,
                    op0=ALU.mult, op1=ALU.add, accum_out=midu(10, 1)),
                  reads=b2r(gsl.start, gsl.stop) + [B_U[2]],
                  writes=[B_U[2], B_U[10]])
                V(lambda e, b=b: e.tensor_tensor(
                    out=carries[:, 16 + b:17 + b], in0=carries[:, 16 + b:17 + b],
                    in1=midu(10, 1), op=ALU.add),
                  reads=[B_U[10], B_CM], writes=[B_CM])

    emit_wmag_prev(0)
    emit_wmag_prev(1)
    emit_lite(0)
    emit_A0(0)
    emit_lite(1)

    V(lambda e: e.tensor_scalar(out=carries[:, 0:24], in0=carries[:, 0:24],
                                scalar1=mask_sb[:, 0:1], scalar2=None, op0=ALU.mult),
      reads=[B_MASK, B_CR, B_CI, B_CM], writes=[B_CR, B_CI, B_CM])
    V(lambda e: e.tensor_scalar(out=carries[:, 16:24], in0=carries[:, 16:24],
                                scalar1=EPS_ACC, scalar2=None, op0=ALU.add),
      reads=[B_CM], writes=[B_CM])

    INV0 = 2 * M


    for pm in range(cfg.NP):
        pbase = pm * M

        dop("sync", "LDP", obuf[:, 0:M],
            ap3(dr["pos_o"], pbase, [[T, 128], [1, M]]), writes=[B_OB0])

        KD = D // 128
        for proj, wdram_, obase, evbias, evfunc in [
                (0, dr["wmag_t"], 0, "v_bmag", AF.Sigmoid),
                (1, dr["wq_t"], 8 * M, "v_bq", AF.Identity)]:
            for m in range(KD):
                p = m % 2
                slot = 2 * proj + p
                ap_w = ap3(wstage, slot * 2048, [[2 * 4096, 128], [1, 1024]])
                dop("gpsimd", f"LDW{slot}", ap_w, wslice_ap(wdram_, m, D, D),
                    writes=[B_WS4[slot]])
                bks = (0, 1) if p == 0 else (2, 3)
                if proj == 1:
                    bks = (4, 5) if p == 0 else (6, 7)

                def mm1(e, slot=slot, bks=bks):
                    last = None
                    for k in range(KD):
                        wt_ap = wstage[:, slot * 2048 + k * 128:
                                       slot * 2048 + (k + 1) * 128]
                        for n in range(2):
                            last = e.matmul(psum[bks[n]][:, 0:NMOV], wt_ap,
                                            mov_xn(k, n), start=(k == 0),
                                            stop=(k == KD - 1))
                    return last
                TE(mm1, reads=[B_WS4[slot], B_XN],
                   writes=[B_PS[bks[0]], B_PS[bks[1]]])
                for n in range(2):
                    S(lambda e, m=m, n=n, bank=bks[n], obase=obase,
                      evbias=evbias, evfunc=evfunc: e.activation(
                        out=big2[:, obase + m * M + n * NMOV:
                                 obase + m * M + n * NMOV + NMOV],
                        in_=psum[bank][:, 0:NMOV], func=evfunc,
                        bias=vec[evbias][:, m:m + 1], scale=1.0),
                      reads=[B_PS[bks[n]], B_VEC[evbias]],
                      writes=b2r(obase + m * M, obase + (m + 1) * M))

        R20, VV0 = M, 2 * M
        B_SQS = [Buf(f"sqs{pm}_0"), Buf(f"sqs{pm}_1")]
        nacc = {0: 0, 1: 0}
        sqi = 0
        for b in range(8):
            for c in range(M // MIDC):
                sl = slice(c * MIDC, (c + 1) * MIDC)
                gsl = slice(b * M + sl.start, b * M + sl.stop)
                V(lambda e, gsl=gsl, b=b: e.tensor_tensor_scan(
                    out=midu(0), data0=big2[:, gsl], data1=big2[:, gsl],
                    initial=carries[:, 16 + b:17 + b], op0=ALU.add, op1=ALU.bypass),
                  reads=b2r(gsl.start, gsl.stop) + [B_CM], writes=[B_U[0]])
                V(lambda e, b=b: e.tensor_copy(
                    out=carries[:, 16 + b:17 + b], in_=midu(0)[:, MIDC - 1:MIDC]),
                  reads=[B_U[0]], writes=[B_CM])
                S(lambda e: e.activation(out=midu(1), in_=midu(0), func=AF.Ln),
                  reads=[B_U[0]], writes=[B_U[1]])
                S(lambda e, gsl=gsl: e.activation(
                    out=obuf_bf[:, INV0 + gsl.start:INV0 + gsl.stop],
                    in_=midu(1), func=AF.Exp, scale=-0.5),
                  reads=[B_U[1]], writes=[B_OBI])
        for b in range(8):
            for c in range(M // MIDC):
                sl = slice(c * MIDC, (c + 1) * MIDC)
                gsl = slice(b * M + sl.start, b * M + sl.stop)
                xcr = b1r(gsl.start, gsl.stop)
                xsr = b1r(8 * M + gsl.start, 8 * M + gsl.stop)
                rrr = b1r(16 * M + gsl.start, 16 * M + gsl.stop)
                rir = b1r(24 * M + gsl.start, 24 * M + gsl.stop)
                V(lambda e, sl=sl, b=b: e.tensor_scalar(
                    out=midu(0), in0=obuf[:, sl], scalar1=vec["v_rp"][:, b:b + 1],
                    scalar2=0.01, op0=ALU.mult, op1=ALU.mult),
                  reads=[B_OB0, B_VEC["v_rp"]], writes=[B_U[0]])
                emit_sincos(0, 1, 2, 3, 4, 6, [])
                V(lambda e, gsl=gsl: e.tensor_tensor(
                    out=midu(1), in0=midu(0), in1=big2[:, 8 * M + gsl.start:
                                                       8 * M + gsl.stop],
                    op=ALU.add),
                  reads=[B_U[0]] + b2r(8 * M + gsl.start, 8 * M + gsl.stop),
                  writes=[B_U[1]])
                emit_sincos(1, 2, 3, 9, 7, 8, [])
                V(lambda e, gsl=gsl: e.tensor_tensor(
                    out=midu_bf(0), in0=big2[:, gsl], in1=xn_hi[:, gsl],
                    op=ALU.mult),
                  reads=b2r(gsl.start, gsl.stop) + [B_XN], writes=[B_U[0]])
                V(lambda e, gsl=gsl: e.tensor_tensor(
                    out=big1[:, gsl], in0=xn_hi[:, gsl], in1=midu_bf(6),
                    op=ALU.mult),
                  reads=[B_XN, B_U[6]], writes=xcr)
                V(lambda e, gsl=gsl: e.tensor_tensor(
                    out=big1[:, 8 * M + gsl.start:8 * M + gsl.stop],
                    in0=xn_hi[:, gsl], in1=midu_bf(4), op=ALU.mult),
                  reads=[B_XN, B_U[4]], writes=xsr)
                V(lambda e: e.tensor_tensor(out=midu_bf(3), in0=midu_bf(0),
                                            in1=midu_bf(6), op=ALU.mult),
                  reads=[B_U[0], B_U[6]], writes=[B_U[3]])
                V(lambda e: e.tensor_tensor(out=midu_bf(9), in0=midu_bf(0),
                                            in1=midu_bf(4), op=ALU.mult),
                  reads=[B_U[0], B_U[4]], writes=[B_U[9]])
                V(lambda e, b=b: e.tensor_tensor_scan(
                    out=midu(10), data0=midu_bf(3), data1=midu_bf(3),
                    initial=carries[:, b:b + 1], op0=ALU.add, op1=ALU.bypass),
                  reads=[B_U[3], B_CR], writes=[B_U[10]])
                V(lambda e, b=b: e.tensor_copy(
                    out=carries[:, b:b + 1], in_=midu(10)[:, MIDC - 1:MIDC]),
                  reads=[B_U[10]], writes=[B_CR])
                V(lambda e, b=b: e.tensor_tensor_scan(
                    out=midu(11), data0=midu_bf(9), data1=midu_bf(9),
                    initial=carries[:, 8 + b:9 + b], op0=ALU.add, op1=ALU.bypass),
                  reads=[B_U[9], B_CI], writes=[B_U[11]])
                V(lambda e, b=b: e.tensor_copy(
                    out=carries[:, 8 + b:9 + b], in_=midu(11)[:, MIDC - 1:MIDC]),
                  reads=[B_U[11]], writes=[B_CI])
                V(lambda e, gsl=gsl: e.scalar_tensor_tensor(
                    out=midu_bf(0), in0=midu(10), scalar=SQRT5,
                    in1=obuf_bf[:, INV0 + gsl.start:INV0 + gsl.stop],
                    op0=ALU.mult, op1=ALU.mult),
                  reads=[B_U[10], B_OBI], writes=[B_U[0]])
                V(lambda e, gsl=gsl: e.scalar_tensor_tensor(
                    out=midu_bf(3), in0=midu(11), scalar=SQRT5,
                    in1=obuf_bf[:, INV0 + gsl.start:INV0 + gsl.stop],
                    op0=ALU.mult, op1=ALU.mult),
                  reads=[B_U[11], B_OBI], writes=[B_U[3]])
                V(lambda e: e.tensor_tensor(out=midu_bf(9), in0=midu_bf(0),
                                            in1=midu_bf(8), op=ALU.mult),
                  reads=[B_U[0], B_U[8]], writes=[B_U[9]])
                V(lambda e: e.tensor_tensor(out=midu_bf(10), in0=midu_bf(3),
                                            in1=midu_bf(7), op=ALU.mult),
                  reads=[B_U[3], B_U[7]], writes=[B_U[10]])
                V(lambda e, gsl=gsl: e.tensor_tensor(
                    out=big1[:, 16 * M + gsl.start:16 * M + gsl.stop],
                    in0=midu_bf(9), in1=midu_bf(10), op=ALU.add),
                  reads=[B_U[9], B_U[10]], writes=rrr)
                V(lambda e: e.tensor_tensor(out=midu_bf(9), in0=midu_bf(3),
                                            in1=midu_bf(8), op=ALU.mult),
                  reads=[B_U[3], B_U[8]], writes=[B_U[9]])
                V(lambda e: e.tensor_tensor(out=midu_bf(11), in0=midu_bf(0),
                                            in1=midu_bf(7), op=ALU.mult),
                  reads=[B_U[0], B_U[7]], writes=[B_U[11]])
                V(lambda e, gsl=gsl: e.tensor_tensor(
                    out=big1[:, 24 * M + gsl.start:24 * M + gsl.stop],
                    in0=midu_bf(9), in1=midu_bf(11), op=ALU.subtract),
                  reads=[B_U[9], B_U[11]], writes=rir)
            for c in range(M // NMOV):
                for g in range(4):
                    cb = g * 8 + b
                    gs = cb * M + c * NMOV
                    breg = b1r(gs, gs + NMOV)
                    bank = 4 + c
                    first = nacc[c] == 0
                    last = nacc[c] == 62
                    nacc[c] += 2
                    TE(lambda e, gs=gs, bank=bank, first=first: e.matmul(
                        psum[bank][0:33, 0:NMOV], sel_a[:], big1[:, gs:gs + NMOV],
                        start=first, stop=False),
                       reads=breg + [B_SELA], writes=[B_PS[bank]])
                    slot = sqi % 2
                    sqi += 1
                    S(lambda e, gs=gs, slot=slot: e.activation(
                        out=big2[:, SQ_BF + slot * NMOV:SQ_BF + slot * NMOV + NMOV],
                        in_=big1[:, gs:gs + NMOV], func=AF.Square),
                      reads=breg, writes=[B_SQS[slot], B_SQ])
                    TE(lambda e, slot=slot, bank=bank, last=last: e.matmul(
                        psum[bank][0:33, 0:NMOV], sel_b[:],
                        big2[:, SQ_BF + slot * NMOV:SQ_BF + slot * NMOV + NMOV],
                        start=False, stop=last),
                       reads=[B_SQS[slot], B_SELB], writes=[B_PS[bank]])
        for c in range(M // NMOV):
            stat_pipeline(NMOV, DH, [], None,
                          r_out=(obuf[:, R20 + c * NMOV:R20 + (c + 1) * NMOV],
                                 [B_OBI]),
                          v_out=(obuf[:, VV0 + c * NMOV:VV0 + (c + 1) * NMOV],
                                 [B_OBI]),
                          sums_bank=4 + c)

        R2B = 6 * M
        for c in range(M // NMOV):
            V(lambda e, c=c: e.tensor_copy(
                out=obuf_bf[:, R2B + c * NMOV:R2B + (c + 1) * NMOV],
                in_=obuf[:, R20 + c * NMOV:R20 + (c + 1) * NMOV]),
              reads=[B_OBI], writes=[B_OBI])
        for c in range(M // NMOV):
            for b in range(32):
                gs = b * M + c * NMOV
                breg = b1r(gs, gs + NMOV)
                V(lambda e, gs=gs, c=c: e.tensor_tensor(
                    out=big1[:, gs:gs + NMOV], in0=big1[:, gs:gs + NMOV],
                    in1=obuf_bf[:, R2B + c * NMOV:R2B + (c + 1) * NMOV],
                    op=ALU.mult),
                  reads=breg + [B_OBI], writes=breg)
                V(lambda e, b=b, c=c: e.tensor_scalar(
                    out=midu_bf(0, NMOV),
                    in0=obuf[:, VV0 + c * NMOV:VV0 + (c + 1) * NMOV],
                    scalar1=vec["v_lng"][:, b:b + 1], scalar2=vec["v_lnb"][:, b:b + 1],
                    op0=ALU.mult, op1=ALU.subtract),
                  reads=[B_OBI, B_VEC["v_lng"], B_VEC["v_lnb"]], writes=[B_U[0]])
                V(lambda e, b=b, gs=gs: e.scalar_tensor_tensor(
                    out=big1[:, gs:gs + NMOV], in0=big1[:, gs:gs + NMOV],
                    scalar=vec["v_lng"][:, b:b + 1], in1=midu_bf(0, NMOV),
                    op0=ALU.mult, op1=ALU.subtract),
                  reads=breg + [B_U[0], B_VEC["v_lng"]], writes=breg)

        def mov_ctx(k, n):
            return big1[:, k * M + n * NMOV:k * M + (n + 1) * NMOV]

        def mov_ctx_bufs(k):
            return b1r(k * M, (k + 1) * M)

        def evac_a1(m, n, bank):
            S(lambda e, m=m, n=n, bank=bank: e.activation(
                out=big2[:, m * M + n * NMOV:m * M + n * NMOV + NMOV],
                in_=psum[bank][:, 0:NMOV], func=AF.Gelu,
                bias=vec["v_b1"][:, m:m + 1], scale=1.0),
              reads=[B_PS[bank], B_VEC["v_b1"]],
              writes=b2r(m * M, (m + 1) * M))
        stream_projection(dr["w1_t"], DH, DA, mov_ctx, mov_ctx_bufs, evac_a1)

        def mov_a1(k, n):
            return big2[:, k * M + n * NMOV:k * M + (n + 1) * NMOV]

        def mov_a1_bufs(k):
            return b2r(k * M, (k + 1) * M)

        def evac_o(m, n, bank):
            S(lambda e, m=m, n=n, bank=bank: e.activation(
                out=obuf[:, m * M + n * NMOV:m * M + n * NMOV + NMOV],
                in_=psum[bank][:, 0:NMOV], func=AF.Identity,
                bias=vec["v_b2"][:, m:m + 1], scale=1.0),
              reads=[B_PS[bank], B_VEC["v_b2"]],
              writes=obr(m * M, (m + 1) * M))
        stream_projection(dr["w2_t"], DA, D, mov_a1, mov_a1_bufs, evac_o)

        WS_K1 = D // 128
        for m in range(DH // 128):
            p = m % 2
            bksA = (0, 1) if p == 0 else (4, 5)
            bksB = (2, 3) if p == 0 else (6, 7)
            slotA, slotB = p, 2 + p
            apA = ap3(wstage, slotA * 2048, [[2 * 4096, 128], [1, 1024]])
            apB = ap3(wstage, slotB * 2048, [[2 * 4096, 128], [1, 1024]])
            dop("gpsimd", f"LDW{slotA}", apA, wslice_ap(dr["fw1_t"], m, D, DH),
                writes=[B_WS4[slotA]])
            dop("gpsimd", f"LDW{slotB}", apB, wslice_ap(dr["fw2_t"], m, D, DH),
                writes=[B_WS4[slotB]])

            def mmA(e, slotA=slotA, bksA=bksA):
                last = None
                for k in range(WS_K1):
                    wt_ap = wstage[:, slotA * 2048 + k * 128:slotA * 2048 + (k + 1) * 128]
                    for n in range(2):
                        last = e.matmul(psum[bksA[n]][:, 0:NMOV], wt_ap,
                                        mov_xn(k, n), start=(k == 0),
                                        stop=(k == WS_K1 - 1))
                return last
            TE(mmA, reads=[B_WS4[slotA], B_XN], writes=[B_PS[bksA[0]], B_PS[bksA[1]]])

            def mmB(e, slotB=slotB, bksB=bksB):
                last = None
                for k in range(WS_K1):
                    wt_ap = wstage[:, slotB * 2048 + k * 128:slotB * 2048 + (k + 1) * 128]
                    for n in range(2):
                        last = e.matmul(psum[bksB[n]][:, 0:NMOV], wt_ap,
                                        mov_xn(k, n), start=(k == 0),
                                        stop=(k == WS_K1 - 1))
                return last
            TE(mmB, reads=[B_WS4[slotB], B_XN], writes=[B_PS[bksB[0]], B_PS[bksB[1]]])
            for n in range(2):
                S(lambda e, m=m, n=n, bksA=bksA: e.activation(
                    out=big1f[:, n * NMOV:(n + 1) * NMOV],
                    in_=psum[bksA[n]][:, 0:NMOV], func=AF.Silu,
                    scale=vec["v_s1"][:, m:m + 1]),
                  reads=[B_PS[bksA[n]], B_VEC["v_s1"]],
                  writes=regs_in(big1_regs, n * NMOV * 4, (n + 1) * NMOV * 4))
                V(lambda e, m=m, n=n, bksB=bksB: e.scalar_tensor_tensor(
                    out=big2[:, m * M + n * NMOV:m * M + n * NMOV + NMOV],
                    in0=psum[bksB[n]][:, 0:NMOV], scalar=vec["v_s2"][:, m:m + 1],
                    in1=big1f[:, n * NMOV:(n + 1) * NMOV],
                    op0=ALU.mult, op1=ALU.mult),
                  reads=[B_PS[bksB[n]], B_VEC["v_s2"]]
                  + regs_in(big1_regs, n * NMOV * 4, (n + 1) * NMOV * 4),
                  writes=b2r(m * M, (m + 1) * M))

        def mov_f(k, n):
            return big2[:, k * M + n * NMOV:k * M + (n + 1) * NMOV]

        def mov_f_bufs(k):
            return b2r(k * M, (k + 1) * M)

        def evac_g1(m, n, bank):
            S(lambda e, m=m, n=n, bank=bank: e.activation(
                out=big1[:, m * M + n * NMOV:m * M + n * NMOV + NMOV],
                in_=psum[bank][:, 0:NMOV], func=AF.Gelu,
                scale=vec["v_h1s"][:, m:m + 1]),
              reads=[B_PS[bank], B_VEC["v_h1s"]],
              writes=b1r(m * M, (m + 1) * M))
        stream_projection(dr["h1_t"], DH, DH, mov_f, mov_f_bufs, evac_g1)

        def mov_g1(k, n):
            return big1[:, k * M + n * NMOV:k * M + (n + 1) * NMOV]

        def mov_g1_bufs(k):
            return b1r(k * M, (k + 1) * M)

        def evac_g2(m, n, bank):
            S(lambda e, m=m, n=n, bank=bank: e.activation(
                out=big2[:, m * M + n * NMOV:m * M + n * NMOV + NMOV],
                in_=psum[bank][:, 0:NMOV], func=AF.Gelu,
                scale=vec["v_h2s"][:, m:m + 1]),
              reads=[B_PS[bank], B_VEC["v_h2s"]],
              writes=b2r(m * M, (m + 1) * M))
        stream_projection(dr["h2_t"], DH, DH, mov_g1, mov_g1_bufs, evac_g2)

        def mov_g2(k, n):
            return big2[:, k * M + n * NMOV:k * M + (n + 1) * NMOV]

        def mov_g2_bufs(k):
            return b2r(k * M, (k + 1) * M)

        def evac_w3(m, n, bank):
            oreg = obr(m * M + n * NMOV, m * M + (n + 1) * NMOV)
            V(lambda e, m=m, n=n, bank=bank: e.scalar_tensor_tensor(
                out=obuf[:, m * M + n * NMOV:m * M + n * NMOV + NMOV],
                in0=psum[bank][:, 0:NMOV], scalar=vec["v_s3"][:, m:m + 1],
                in1=obuf[:, m * M + n * NMOV:m * M + n * NMOV + NMOV],
                op0=ALU.mult, op1=ALU.add),
              reads=[B_PS[bank], B_VEC["v_s3"]] + oreg, writes=oreg)
        stream_projection(dr["w3_t"], DH, D, mov_g2, mov_g2_bufs, evac_w3)

        if pm + 1 < cfg.NP:
            emit_A0(pm + 1)

        OSTF = 4 * M
        for c in range(M // XC):
            buf = c % 2
            ldx = dop("sync", f"LDS{buf}", xst_ap(buf, XC),
                      xdram_ap(dr["x_fm"], pbase + c * XC, XC), writes=B_XS[buf])
            for b in range(8):
                oreg = obr(b * M + c * XC, b * M + (c + 1) * XC)
                V(lambda e, b=b, buf=buf, c=c, pbase=pbase: e.tensor_tensor(
                    out=midu(0, XC), in0=xst_blk(buf, b, XC),
                    in1=lnbmu[:, pbase + c * XC:pbase + (c + 1) * XC],
                    op=ALU.subtract),
                  reads=B_XS[buf] + [B_LNM], writes=[B_U[0]], extra=[ldx])
                V(lambda e, c=c, pbase=pbase: e.tensor_tensor(
                    out=midu(1, XC), in0=midu(0, XC),
                    in1=lnbr[:, pbase + c * XC:pbase + (c + 1) * XC],
                    op=ALU.mult),
                  reads=[B_U[0], B_LNR], writes=[B_U[1]])
                V(lambda e, b=b, buf=buf: e.tensor_tensor(
                    out=midu(0, XC), in0=midu(1, XC), in1=xst_blk(buf, b, XC),
                    op=ALU.add),
                  reads=[B_U[1]] + B_XS[buf], writes=[B_U[0]])
                V(lambda e, b=b, c=c, buf=buf: e.tensor_tensor(
                    out=big1f[:, OSTF + buf * 8 * XC + b * XC:
                              OSTF + buf * 8 * XC + (b + 1) * XC],
                    in0=midu(0, XC),
                    in1=obuf[:, b * M + c * XC:b * M + (c + 1) * XC], op=ALU.add),
                  reads=[B_U[0]] + oreg, writes=B_OS[buf])
            gc = (pbase + c * XC) // XC
            dop("sync", f"STS{buf}",
                ap3(dr["y_fm"], gc * (128 * 8 * XC), [[8 * XC, 128], [1, 8 * XC]]),
                ap3(big1f, OSTF + buf * 8 * XC, [[16 * M, 128], [1, 8 * XC]]),
                reads=B_OS[buf])

    t0_, t1_ = cnt["STS0"], cnt["STS1"]
    q["sync"].append(lambda e: e.wait_ge(sems["STS0"], t0_))
    q["sync"].append(lambda e: e.wait_ge(sems["STS1"], t1_))

    block = es.enter_context(nc.Block())

    @block.sync
    def _(e):
        for f in q["sync"]:
            f(e)

    @block.gpsimd
    def _(e):
        for f in q["gpsimd"]:
            f(e)

    @block.vector
    def _(e):
        for f in q["vector"]:
            f(e)

    @block.scalar
    def _(e):
        for f in q["scalar"]:
            f(e)

    @block.tensor
    def _(e):
        for f in q["tensor"]:
            f(e)

    es.close()
    return nc


def _layout_inputs(ins, T=2048):
    f32 = np.float32
    x = np.asarray(ins["L0_attn_x"], dtype=f32)
    B, Sq, Dm = x.shape
    H = Sq // 2
    def pack_w(a):
        wt_ = np.asarray(a, f32).T
        IN, OUT = wt_.shape
        return np.ascontiguousarray(
            wt_.reshape(IN // 128, 128, OUT // 128, 128).transpose(2, 1, 0, 3)
        ).reshape(OUT, IN)

    wt = {
        "wmag_t": pack_w(ins["L0_attn_W_mag"]),
        "wq_t": pack_w(ins["L0_attn_W_q"]),
        "w1_t": pack_w(ins["L0_attn_W1"]),
        "w2_t": pack_w(ins["L0_attn_W2"]),
        "fw1_t": pack_w(ins["L0_ffn_w1_w"]),
        "fw2_t": pack_w(ins["L0_ffn_w2_w"]),
        "h1_t": pack_w(ins["L0_ffn_h1_w"]),
        "h2_t": pack_w(ins["L0_ffn_h2_w"]),
        "w3_t": pack_w(ins["L0_ffn_w3_w"]),
    }

    def v128(name, w):
        return np.ascontiguousarray(np.asarray(ins[name], f32).reshape(w, 128).T)

    vecs = {
        "v_bmag": v128("L0_attn_b_mag", 8), "v_bq": v128("L0_attn_b_q", 8),
        "v_b2": v128("L0_attn_b2", 8), "v_rp": v128("L0_attn_random_phases", 8),
        "v_s3": v128("L0_ffn_w3_s", 8), "v_b1": v128("L0_attn_b1", 16),
        "v_lng": v128("L0_attn_ln_g", 32), "v_lnb": v128("L0_attn_ln_b", 32),
        "v_s1": v128("L0_ffn_w1_s", 32), "v_s2": v128("L0_ffn_w2_s", 32),
        "v_h1s": v128("L0_ffn_h1_s", 32), "v_h2s": v128("L0_ffn_h2_s", 32),
    }
    ones2 = np.ones((33, 128), dtype=np.float32)
    sel_a = np.zeros((128, 33), dtype=ml_dtypes.bfloat16)
    sel_a[:, 0] = 1
    sel_b = np.zeros((128, 33), dtype=ml_dtypes.bfloat16)
    sel_b[:, 32] = 1
    pos = np.arange(Sq, dtype=f32)

    in_maps = []
    for core in range(8):
        b, h = core // 2, core % 2
        def pack_x(xt):
            XC = 256
            return np.ascontiguousarray(
                xt.reshape(8, 128, H // XC, XC).transpose(2, 1, 0, 3)
            ).reshape(H // XC, 128 * 8 * XC)

        m = {
            "x_fm": pack_x(np.ascontiguousarray(x[b, h * H:(h + 1) * H].T)),
            "xp_fm": pack_x(np.ascontiguousarray(x[b, 0:H].T)),
            "pos_o": np.ascontiguousarray(
                np.broadcast_to(pos[h * H:(h + 1) * H], (128, H))),
            "pos_p": np.ascontiguousarray(np.broadcast_to(pos[0:H], (128, H))),
            "mask": np.full((128, 1), 1.0 if h == 1 else 0.0, dtype=f32),
            "ones2": ones2, "sel_a": sel_a, "sel_b": sel_b,
        }
        m.update(wt)
        m.update(vecs)
        in_maps.append(m)
    return in_maps


def kernel(**ins):
    in_maps = _layout_inputs(ins)
    nc = build_program(CFG())
    res = run_bass_kernel_spmd(nc, in_maps, core_ids=list(range(8)))
    x = np.asarray(ins["L0_attn_x"])
    B, Sq, Dm = x.shape
    H = Sq // 2
    out = np.empty((B, Sq, Dm), dtype=np.float32)
    XC = 256
    for core in range(8):
        b, h = core // 2, core % 2
        y4 = res.results[core]["y_fm"].reshape(H // XC, 128, 8, XC)
        y_fm = y4.transpose(2, 1, 0, 3).reshape(Dm, H)
        out[b, h * H:(h + 1) * H] = y_fm.T
    return out
